# version 01
# baseline (speedup 1.0000x reference)
import numpy as np
import ml_dtypes
from contextlib import ExitStack

import concourse.bass as bass
import concourse.mybir as mybir
from concourse.bass_utils import run_bass_kernel_spmd

F32 = mybir.dt.float32
BF16 = mybir.dt.bfloat16
I32 = mybir.dt.int32
OP = mybir.AluOpType
AF = mybir.ActivationFunctionType
AX = mybir.AxisListType

NCORES = 8
NTOK = 2048
D = 1024
TT = NTOK // 128
DC = D // 128
DH = 4096
HC = DH // 128
TCH = 256
NTC = NTOK // TCH
LN_EPS = 1e-5
LAM_INIT = float(0.8 - 0.6 * np.exp(-0.3 * 12))

SUBLAYERS = 4


class EngProxy:

    def __init__(self, eng):
        self._e = eng
        self.last = None

    def __getattr__(self, name):
        f = getattr(self._e, name)
        if not callable(f):
            return f

        def call(*a, **k):
            r = f(*a, **k)
            if r is not None and hasattr(r, "then_inc"):
                self.last = r
            return r
        return call


class Plan:
    ENGS = ("sync", "vector", "scalar", "tensor", "gpsimd")

    def __init__(self):
        self.ops = {e: [] for e in self.ENGS}
        self.cnt = {}
        self.sems = {}

    def _c(self, n):
        return self.cnt.setdefault(n, 0)

    def inc(self, name, n=1):
        self.cnt[name] = self._c(name) + n
        return self.cnt[name]

    def val(self, name):
        return self._c(name)

    def op(self, eng, build, inc_sem=None, inc_n=1, waits=()):
        waits = tuple((s, v) for (s, v) in waits if v > 0)
        if inc_sem is not None:
            self.inc(inc_sem, inc_n)

        def fn(e, _b=build, _w=waits, _s=inc_sem, _n=inc_n):
            for (s, v) in _w:
                e.wait_ge(self.sems[s], v)
            pr = EngProxy(e)
            inst = _b(pr)
            if inst is None or not hasattr(inst, "then_inc"):
                inst = pr.last
            if _s is not None:
                if inst is None:
                    inst = e.nop()
                inst.then_inc(self.sems[_s], _n)

        self.ops[eng].append(fn)

    def barrier(self):
        vals = {}
        for e in self.ENGS:
            vals[e] = self.inc("bar_" + e)
        for e in self.ENGS:
            def fn(eobj, me=e, vals=dict(vals)):
                if me != "sync":
                    eobj.drain()
                eobj.sem_inc(self.sems["bar_" + me], 1)
                for o in self.ENGS:
                    eobj.wait_ge(self.sems["bar_" + o], vals[o])
            self.ops[e].append(fn)


def build_nc():
    nc = bass.Bass("TRN2", target_bir_lowering=False)
    dr = {}

    def din(name, shape, dt=F32):
        dr[name] = nc.dram_tensor(name, shape, dt, kind="ExternalInput").ap()

    din("x", [NTOK, D])
    din("wqkv", [D, 3 * D]); din("wouta", [D, D])
    din("wq", [D, D]); din("wk", [D, D]); din("wv", [D, D]); din("wo", [D, D])
    din("lqk", [1, 256]); din("lng", [1, 128]); din("lnb", [1, 128])
    din("w1", [DH, D]); din("convw", [128, 96]); din("convb", [128, 32])
    din("w2", [D, DH])
    din("cid", [128, 128], BF16)
    din("cmask", [128, 896], BF16)
    din("chalf", [128, 2], BF16)
    din("conesf", [1, 128], F32)
    din("conescol", [128, 1], F32)
    dr["out"] = nc.dram_tensor("out", [NTOK, D], F32, kind="ExternalOutput").ap()
    dr["w1t"] = nc.dram_tensor("w1t", [D, DH], BF16, kind="Internal").ap()
    dr["w2t"] = nc.dram_tensor("w2t", [DH, D], BF16, kind="Internal").ap()

    semnames = ["ldin", "ldw", "ldwa", "ldwb", "ldwc", "ldwd", "ldst",
                "sto", "sT", "sV", "sS", "sG"] + \
               ["bar_" + e for e in Plan.ENGS]
    p = Plan()

    with ExitStack() as stack:
        E = stack.enter_context
        p.sems = {n: E(nc.semaphore(n)) for n in semnames}

        g = {"nc": nc, "dr": dr}
        g["X"] = E(nc.sbuf_tensor("X", [128, TT * D], F32))
        g["CID"] = E(nc.sbuf_tensor("CID", [128, 128], BF16))
        g["CHALF"] = E(nc.sbuf_tensor("CHALF", [128, 2], BF16))
        g["CONESF"] = E(nc.sbuf_tensor("CONESF", [1, 128], F32))
        g["CONESCOL"] = E(nc.sbuf_tensor("CONESCOL", [128, 1], F32))
        g["GLNG"] = E(nc.sbuf_tensor("GLNG", [128, 128], F32))
        g["GLNB"] = E(nc.sbuf_tensor("GLNB", [128, 128], F32))
        g["LAMN"] = E(nc.sbuf_tensor("LAMN", [128, 1], F32))
        g["LQK"] = E(nc.sbuf_tensor("LQK", [1, 256], F32))
        g["LROW"] = E(nc.sbuf_tensor("LROW", [1, 256], F32))
        g["SSTAT"] = E(nc.sbuf_tensor("SSTAT", [128, 64], F32))
        g["FSC"] = E(nc.sbuf_tensor("FSC", [128, 48], F32))
        g["LCOL"] = E(nc.sbuf_tensor("LCOL", [128, 24], F32))
        g["SR2"] = E(nc.sbuf_tensor("SR2", [32, 256], F32))
        g["PERS"] = E(nc.sbuf_tensor("PERS", [128, 8], F32))
        g["CONVP"] = E(nc.sbuf_tensor("CONVP", [128, 96], F32))
        g["CONVB"] = E(nc.sbuf_tensor("CONVB", [128, 32], F32))
        g["TMPA"] = E(nc.sbuf_tensor("TMPA", [128, 1024], F32))
        g["TMPB"] = E(nc.sbuf_tensor("TMPB", [128, 1024], BF16))
        g["ptv"] = {"v": 0}
        g["pssv"] = {"v": 0}
        g["PS"] = [E(nc.psum_tensor(f"ps{i}", [128, 512], F32)) for i in range(6)]
        g["PT"] = E(nc.psum_tensor("pt", [128, 1024], BF16))
        g["PSS"] = E(nc.psum_tensor("pss", [128, 512], F32))

        xap = bass.AP(dr["x"].tensor, 0, [[D, 128], [128 * D, TT], [1, D]])
        p.op("sync", lambda s: s.dma_start(g["X"][:], xap), "ldin", 16)
        for nm, dst in (("cid", "CID"), ("chalf", "CHALF"), ("conesf", "CONESF"),
                        ("conescol", "CONESCOL"), ("lqk", "LQK"),
                        ("convw", "CONVP"), ("convb", "CONVB")):
            p.op("sync", lambda s, nm=nm, d=dst: s.dma_start(g[d][:], dr[nm]),
                 "ldin", 16)
        p.op("sync", lambda s: s.dma_start(g["LROW"][0:1, 0:128], dr["lng"]), "ldin", 16)
        p.op("sync", lambda s: s.dma_start(g["LROW"][0:1, 128:256], dr["lnb"]), "ldin", 16)
        la = p.val("ldin")
        for e in ("vector", "scalar", "tensor", "gpsimd"):
            p.op(e, lambda x: None, waits=[("ldin", la)])
        p.op("vector", lambda v: (v.memset(g["SR2"][:], 0.0), v.drain())[1], "sV")

        build_attn(p, g, which="a")
        if SUBLAYERS >= 2:
            build_ffn(p, g, first=True)
        if SUBLAYERS >= 3:
            build_attn(p, g, which="b")
        if SUBLAYERS >= 4:
            build_ffn(p, g, first=False)

        p.barrier()
        oap = bass.AP(dr["out"].tensor, 0, [[D, 128], [128 * D, TT], [1, D]])
        p.op("sync", lambda s: s.dma_start(oap, g["X"][:]), "sto", 16)
        p.op("sync", lambda s: None, waits=[("sto", p.val("sto"))])

        with nc.Block() as block:
            @block.sync
            def _(eng):
                for f in p.ops["sync"]:
                    f(eng)

            @block.vector
            def _(eng):
                for f in p.ops["vector"]:
                    f(eng)

            @block.scalar
            def _(eng):
                for f in p.ops["scalar"]:
                    f(eng)

            @block.tensor
            def _(eng):
                for f in p.ops["tensor"]:
                    f(eng)

            @block.gpsimd
            def _(eng):
                for f in p.ops["gpsimd"]:
                    f(eng)

    return nc



def bcast_row(p, g, dst, srcrow, n, vwait):
    PSS = g["PSS"]
    pw = max(vwait, g["pssv"]["v"])

    def mm(te, pw=pw, srcrow=srcrow, n=n):
        te.wait_ge(p.sems["sV"], pw)
        te.matmul(PSS[:, 0:n], g["CONESF"][0:1, :], srcrow, start=True, stop=True)
    p.op("tensor", mm, "sT")
    tv = p.val("sT")
    p.op("vector", lambda v, dst=dst, n=n: (v.tensor_copy(dst, PSS[:, 0:n]),
                                            v.drain())[1],
         "sV", waits=[("sT", tv)])
    g["pssv"]["v"] = p.val("sV")
    return p.val("sV")

def build_ln_stats(p, g):
    for j in range(TT):
        p.op("vector", lambda v, j=j: v.tensor_reduce(
            g["SSTAT"][:, j:j + 1], g["X"][:, j * D:(j + 1) * D], AX.X, OP.add))
        p.op("scalar", lambda s, j=j: (s.activation(
            g["TMPB"][:], g["X"][:, j * D:(j + 1) * D], AF.Square,
            accum_out=g["SSTAT"][:, 16 + j:17 + j]), s.drain())[1])
    p.op("vector", lambda v: v.drain(), "sV")
    p.op("scalar", lambda s: s.drain(), "sS")
    w = [("sV", p.val("sV")), ("sS", p.val("sS"))]

    def stats(v):
        S = g["SSTAT"]
        v.tensor_scalar(S[:, 0:16], S[:, 0:16], 1.0 / D, None, OP.mult)
        v.tensor_scalar(S[:, 16:32], S[:, 16:32], 1.0 / D, None, OP.mult)
        v.drain()
        v.tensor_mul(S[:, 32:48], S[:, 0:16], S[:, 0:16])
        v.drain()
        v.tensor_sub(S[:, 16:32], S[:, 16:32], S[:, 32:48])
        v.drain()
    p.op("vector", stats, "sV", waits=w)

    def addeps(v):
        v.tensor_scalar(g["SSTAT"][:, 16:32], g["SSTAT"][:, 16:32], LN_EPS, None,
                        OP.add)
        v.drain()
    p.op("vector", addeps, "sV")
    p.op("scalar", lambda s: s.activation(
        g["SSTAT"][:, 32:48], g["SSTAT"][:, 16:32], AF.Sqrt),
        "sS", waits=[("sV", p.val("sV"))])

    def fin(v):
        S = g["SSTAT"]
        v.reciprocal(S[:, 32:48], S[:, 32:48])
        v.drain()
        v.tensor_mul(S[:, 48:64], S[:, 0:16], S[:, 32:48])
        v.drain()
        v.tensor_scalar(S[:, 48:64], S[:, 48:64], -1.0, None, OP.mult)
        v.drain()
    p.op("vector", fin, "sV", waits=[("sS", p.val("sS"))])
    return p.val("sV")


def transpose_to(p, g, XD, j, ncols, col_stride, base_off, vwait):
    def tr(t, vwait=max(vwait, g["ptv"]["v"]), ncols=ncols):
        t.wait_ge(p.sems["sV"], vwait)
        for c in range(ncols):
            t.transpose(g["PT"][:, c * 128:(c + 1) * 128],
                        g["TMPB"][:, c * 128:(c + 1) * 128], g["CID"][:])
    p.op("tensor", tr, "sT")
    tv = p.val("sT")
    dst = bass.AP(XD, base_off + j * 128,
                  [[XD.ap().ap[0][0], 128], [col_stride, ncols], [1, 128]])
    src = bass.AP(g["PT"], 0, [[g["PT"].ap().ap[0][0], 128], [128, ncols], [1, 128]])
    p.op("vector", lambda v, dst=dst, src=src: v.tensor_copy(dst, src),
         "sV", waits=[("sT", tv)])
    g["ptv"]["v"] = p.val("sV")
    return p.val("sV")


def build_attn(p, g, which):
    nc, dr = g["nc"], g["dr"]
    causal = (which == "a")
    PS, PT, PSS, CID = g["PS"], g["PT"], g["PSS"], g["CID"]
    gw = 65 if causal else 129
    ng = 16 if causal else 8
    stride = 1040 if causal else 1032

    p.barrier()

    with ExitStack() as stack:
        E = stack.enter_context
        XLNT = E(nc.sbuf_tensor("XLNT" + which, [128, DC * NTOK], BF16))
        VB = E(nc.sbuf_tensor("VB" + which, [128, TT * stride], BF16))
        OPRIME = E(nc.sbuf_tensor("OPR" + which, [128, DC * NTOK], BF16))
        QC = E(nc.sbuf_tensor("QCh" + which, [128, NTOK], BF16))
        KC = E(nc.sbuf_tensor("KCh" + which, [128, NTOK], BF16))
        PPA = E(nc.sbuf_tensor("PPA" + which, [128, 2 * 512], BF16))
        PPB = E(nc.sbuf_tensor("PPB" + which, [128, 2 * 512], BF16))
        WSA = E(nc.sbuf_tensor("WSA" + which, [128, 1024], F32))
        ZB = E(nc.sbuf_tensor("ZB" + which, [128, 512], BF16))
        WVB = E(nc.sbuf_tensor("WVB" + which, [128, DC * D], BF16))
        WSB = WVB
        CMASK = E(nc.sbuf_tensor("CMSK" + which, [128, 896], BF16)) \
            if causal else None

        if causal:
            p.op("sync", lambda s: s.dma_start(CMASK[:], dr["cmask"]), "ldin", 16)
            p.op("vector", lambda v: None, waits=[("ldin", p.val("ldin"))])

        statv = build_ln_stats(p, g)
        for j in range(TT):
            p.op("vector", lambda v, j=j: v.tensor_scalar(
                g["TMPB"][:], g["X"][:, j * D:(j + 1) * D],
                g["SSTAT"][:, 32 + j:33 + j], g["SSTAT"][:, 48 + j:49 + j],
                OP.mult, OP.add), "sV", waits=[("sV", statv)])
            transpose_to(p, g, XLNT, j, DC, NTOK, 0, p.val("sV"))
        xlnt_done = p.val("sV")

        if which == "b":
            build_b_smalls(p, g)

        ones_ap = bass.AP(VB, 0, [[VB.ap().ap[0][0], 128], [stride, TT], [gw, ng], [1, 1]])
        p.op("vector", lambda v: v.memset(ones_ap, 1.0), "sV")
        p.op("vector", lambda v: (v.memset(ZB[:], 0.0), v.drain())[1], "sV")

        wv_t = (dr["wqkv"] if causal else dr["wv"]).tensor
        wv_w = 3 * D if causal else D
        wv_off = 2 * D if causal else 0
        for d in range(DC):
            lv = p.val("sV")
            p.op("sync", lambda s, d=d: s.dma_start(
                WSA[:], bass.AP(wv_t, d * 128 * wv_w + wv_off,
                                [[wv_w, 128], [1, D]])), "ldw", 16,
                waits=[("sV", lv)])
            p.op("vector", lambda v, d=d: v.tensor_copy(
                WVB[:, d * D:(d + 1) * D], WSA[:]), "sV",
                waits=[("ldw", 16 * p.val("ldw") // 16)])
        vcast = p.val("sV")

        for t in range(TT):
            for half in range(2):
                pb = PS[4 + half]
                tw = [("sV", max(vcast, xlnt_done)), ("sS", p.val("sS"))]

                def mm(te, t=t, half=half, pb=pb):
                    for d in range(DC):
                        te.matmul(pb[:, 0:512],
                                  XLNT[:, d * NTOK + t * 128:d * NTOK + (t + 1) * 128],
                                  WVB[:, d * D + half * 512:d * D + (half + 1) * 512],
                                  start=(d == 0), stop=(d == DC - 1))
                p.op("tensor", mm, "sT", waits=tw)
                tv = p.val("sT")
                if causal:
                    dst = bass.AP(VB, t * 1040 + half * 8 * 65 + 1,
                                  [[VB.ap().ap[0][0], 128], [65, 8], [1, 64]])
                else:
                    dst = bass.AP(VB, t * 1032 + half * 4 * 129 + 1,
                                  [[VB.ap().ap[0][0], 128], [129, 4], [1, 128]])
                p.op("scalar", lambda s, dst=dst, pb=pb: s.activation(
                    dst, pb[:, 0:512], AF.Copy), "sS", waits=[("sT", tv)])

        wq_t = (dr["wqkv"] if causal else dr["wq"]).tensor
        wk_t = (dr["wqkv"] if causal else dr["wk"]).tensor
        ww = 3 * D if causal else D
        wk_off = D if causal else 0

        for c in range(DC):
            for (wt, woff, dsto) in ((wq_t, 0, 0), (wk_t, wk_off, 1024)):
                lv = p.val("sV")
                p.op("sync", lambda s, wt=wt, woff=woff, c=c: s.dma_start(
                    WSA[:], bass.AP(wt, woff + c * 128,
                                    [[ww, 128], [128 * ww, DC], [1, 128]])),
                    "ldw", 16, waits=[("sV", lv)])
                p.op("vector", lambda v, dsto=dsto: v.tensor_copy(
                    WSB[:, dsto:dsto + 1024], WSA[:]), "sV",
                    waits=[("ldw", 16 * p.val("ldw") // 16), ("sT", p.val("sT"))])
            wcast = p.val("sV")

            for (dst, wo, scale, bank) in ((QC, 0, 0.125, 0), (KC, 1024, 1.0, 1)):
                for tc in range(4):
                    sw = [("sV", wcast), ("sS", p.val("sS"))]

                    def mm(te, wo=wo, tc=tc, bank=bank):
                        for d in range(DC):
                            te.matmul(PS[bank][:, 0:512],
                                      WSB[:, wo + d * 128:wo + (d + 1) * 128],
                                      XLNT[:, d * NTOK + tc * 512:d * NTOK + (tc + 1) * 512],
                                      start=(d == 0), stop=(d == DC - 1))
                    p.op("tensor", mm, "sT", waits=sw)
                    tv = p.val("sT")
                    p.op("scalar", lambda s, dst=dst, tc=tc, scale=scale, bank=bank:
                         s.activation(dst[:, tc * 512:(tc + 1) * 512],
                                      PS[bank][:, 0:512], AF.Copy, scale=scale),
                         "sS", waits=[("sT", tv)])
            qk_done = p.val("sS")

            build_head_bound(p, g, QC, KC, PPA, PPB, qk_done)
            bias_ready = p.val("sV")

            if causal:
                qcs = [(qc * 512, 4, list(range(4 * qc + 4))) for qc in range(4)]
            else:
                qcs = [(i * 384, 3, list(range(16))) for i in range(5)] + \
                      [(5 * 384, 1, list(range(16)))]

            for (q0, nqt, kts) in qcs:
                qw = nqt * 128
                zw = [("sS", p.val("sS")), ("sV", p.val("sV"))]

                def zmm(te):
                    te.matmul(PS[4][:, 0:512], g["CID"][:], ZB[:], start=True, stop=True)
                    te.matmul(PS[5][:, 0:512], g["CID"][:], ZB[:], start=True, stop=True)
                p.op("tensor", zmm, "sT", waits=zw)
                for ki, kt in enumerate(kts):
                    bA, bB = PS[ki % 2], PS[2 + ki % 2]
                    sready = [("sS", p.val("sS")), ("sV", max(p.val("sV"), bias_ready))]

                    def smm(te, kt=kt, q0=q0, qw=qw, bA=bA, bB=bB):
                        te.matmul(bA[:, 0:qw], KC[0:64, kt * 128:(kt + 1) * 128],
                                  QC[0:64, q0:q0 + qw], start=True, stop=True)
                        te.matmul(bB[:, 0:qw], KC[64:128, kt * 128:(kt + 1) * 128],
                                  QC[64:128, q0:q0 + qw], start=True, stop=True,
                                  tile_position=(64, 0))
                    p.op("tensor", smm, "sT", waits=sready)
                    tvS = p.val("sT")
                    par = ki % 2
                    for (pp, bb, col) in ((PPA, bA, 16), (PPB, bB, 17)):
                        p.op("scalar", lambda s, pp=pp, bb=bb, col=col, par=par, qw=qw:
                             s.activation(pp[:, par * 512:par * 512 + qw], bb[:, 0:qw],
                                          AF.Exp, bias=g["LCOL"][:, col:col + 1]),
                             "sS", waits=[("sT", tvS)])
                    expv = p.val("sS")
                    mv = 0
                    if causal and kt >= (q0 // 128):
                        r = kt - q0 // 128
                        def msk(v, par=par, qw=qw, r=r):
                            for pp in (PPA, PPB):
                                v.tensor_mul(pp[:, par * 512:par * 512 + qw],
                                             pp[:, par * 512:par * 512 + qw],
                                             CMASK[:, 384 - 128 * r:384 - 128 * r + qw])
                            v.drain()
                        p.op("vector", msk, "sV", waits=[("sS", expv)])
                        mv = p.val("sV")
                    pvw = [("sS", expv)] + ([("sV", mv)] if mv else [])

                    def pv(te, kt=kt, nqt=nqt, par=par, ki=ki, nk=len(kts), c=c,
                           causal=causal, gw=gw):
                        for qt in range(nqt):
                            for (h, pp, ob) in ((0, PPA, PS[4]), (1, PPB, PS[5])):
                                if causal:
                                    vo = kt * 1040 + (2 * c + h) * 65
                                else:
                                    vo = kt * 1032 + c * 129
                                te.matmul(ob[:, qt * gw:qt * gw + gw],
                                          pp[:, par * 512 + qt * 128:
                                             par * 512 + (qt + 1) * 128],
                                          VB[:, vo:vo + gw],
                                          start=False, stop=False,
                                          skip_group_check=True)
                    p.op("tensor", pv, "sT", waits=pvw)
                pv_done = p.val("sT")

                if causal:
                    attn_a_epilogue(p, g, OPRIME, c, q0, nqt, pv_done)
                else:
                    attn_b_epilogue(p, g, OPRIME, c, q0, nqt, pv_done)

        wo_t = (dr["wouta"] if causal else dr["wo"]).tensor
        for d in range(DC):
            lv = p.val("sV")
            p.op("sync", lambda s, d=d: s.dma_start(
                WSA[:], bass.AP(wo_t, d * 128 * D, [[D, 128], [1, D]])),
                "ldw", 16, waits=[("sV", lv), ("sT", p.val("sT"))])
            p.op("vector", lambda v, d=d: v.tensor_copy(
                WVB[:, d * D:(d + 1) * D], WSA[:]), "sV",
                waits=[("ldw", 16 * p.val("ldw") // 16)])
        wocast = p.val("sV")

        for oc in range(DC):
            for tc in range(4):
                bank = PS[(oc * 4 + tc) % 2]
                mmw = [("sV", max(wocast, p.val("sV"))), ("sS", p.val("sS"))]

                def mm(te, oc=oc, tc=tc, bank=bank):
                    for ch in range(DC):
                        te.matmul(bank[:, 0:512],
                                  WVB[:, ch * D + oc * 128:ch * D + (oc + 1) * 128],
                                  OPRIME[:, ch * NTOK + tc * 512:ch * NTOK + (tc + 1) * 512],
                                  start=(ch == 0), stop=(ch == DC - 1))
                p.op("tensor", mm, "sT", waits=mmw)
                tv = p.val("sT")
                p.op("scalar", lambda s, bank=bank: s.activation(
                    g["TMPB"][:, 0:512], bank[:, 0:512], AF.Copy), "sS",
                    waits=[("sT", tv)])
                sv = p.val("sS")

                def tr(te, sv=sv, pw=g["ptv"]["v"]):
                    te.wait_ge(p.sems["sS"], sv)
                    te.wait_ge(p.sems["sV"], pw)
                    for i in range(4):
                        te.transpose(PT[:, i * 128:(i + 1) * 128],
                                     g["TMPB"][:, i * 128:(i + 1) * 128], CID[:])
                p.op("tensor", tr, "sT")
                tv2 = p.val("sT")

                def add(v, oc=oc, tc=tc):
                    for i in range(4):
                        j = tc * 4 + i
                        sl = g["X"][:, j * D + oc * 128:j * D + (oc + 1) * 128]
                        v.tensor_add(sl, sl, PT[:, i * 128:(i + 1) * 128])
                    v.drain()
                p.op("vector", add, "sV", waits=[("sT", tv2)])
                g["ptv"]["v"] = p.val("sV")


def attn_a_epilogue(p, g, OPRIME, c, q0, nqt, pv_done):
    PS, PT, CID = g["PS"], g["PT"], g["CID"]
    for (h, ob) in ((0, PS[4]), (1, PS[5])):
        lap = bass.AP(ob, 0, [[ob.ap().ap[0][0], 128], [65, nqt], [1, 1]])
        def rec(v, lap=lap, h=h, nqt=nqt):
            v.reciprocal(g["LCOL"][:, h * 4:h * 4 + nqt], lap)
            v.drain()
        p.op("vector", rec, "sV", waits=[("sT", pv_done)])
    rv = p.val("sV")
    for qt in range(nqt):
        for (h, ob) in ((0, PS[4]), (1, PS[5])):
            p.op("scalar", lambda s, qt=qt, h=h, ob=ob: s.activation(
                g["TMPB"][:, (h * nqt + qt) * 64:(h * nqt + qt + 1) * 64],
                ob[:, qt * 65 + 1:qt * 65 + 65], AF.Copy,
                scale=g["LCOL"][:, h * 4 + qt:h * 4 + qt + 1]),
                "sS", waits=[("sV", rv)])
    sv = p.val("sS")

    def tr(te, sv=sv, nqt=nqt, pw=g["ptv"]["v"]):
        te.wait_ge(p.sems["sS"], sv)
        te.wait_ge(p.sems["sV"], pw)
        for k in range(2 * nqt):
            te.transpose(PT[0:64, k * 128:(k + 1) * 128],
                         g["TMPB"][:, k * 64:(k + 1) * 64], CID[:])
    p.op("tensor", tr, "sT")
    tv = p.val("sT")

    def ev(v, c=c, q0=q0, nqt=nqt):
        for h in range(2):
            for qt in range(nqt):
                k = h * nqt + qt
                v.tensor_copy(
                    OPRIME[h * 64:h * 64 + 64,
                           c * NTOK + q0 + qt * 128:c * NTOK + q0 + (qt + 1) * 128],
                    PT[0:64, k * 128:(k + 1) * 128])
        v.drain()
    p.op("vector", ev, "sV", waits=[("sT", tv)])
    g["ptv"]["v"] = p.val("sV")


def attn_b_epilogue(p, g, OPRIME, c, q0, nqt, pv_done):
    PS, PT, CID = g["PS"], g["PT"], g["CID"]
    LCOL, TMPA, TMPB = g["LCOL"], g["TMPA"], g["TMPB"]
    U0, U1 = PS[4], PS[5]
    for (i, ub) in ((0, U0), (1, U1)):
        lap = bass.AP(ub, 0, [[ub.ap().ap[0][0], 128], [129, nqt], [1, 1]])
        def rec(v, lap=lap, i=i, nqt=nqt):
            v.reciprocal(LCOL[:, i * 4:i * 4 + nqt], lap)
            v.drain()
        p.op("vector", rec, "sV", waits=[("sT", pv_done)])

    def s1(v, nqt=nqt):
        v.tensor_scalar(LCOL[:, 4:4 + nqt], LCOL[:, 4:4 + nqt],
                        g["LAMN"][:, 0:1], None, OP.mult)
        v.drain()
    p.op("vector", s1, "sV")
    rv = p.val("sV")

    for qt in range(nqt):
        p.op("scalar", lambda s, qt=qt: s.activation(
            TMPA[:, qt * 128:(qt + 1) * 128], U0[:, qt * 129 + 1:qt * 129 + 129],
            AF.Copy, scale=LCOL[:, qt:qt + 1]), "sS", waits=[("sV", rv)])
    s0v = p.val("sS")
    for qt in range(nqt):
        p.op("vector", lambda v, qt=qt: v.scalar_tensor_tensor(
            TMPA[:, qt * 128:(qt + 1) * 128], U1[:, qt * 129 + 1:qt * 129 + 129],
            LCOL[:, 4 + qt:5 + qt], TMPA[:, qt * 128:(qt + 1) * 128],
            OP.mult, OP.add), "sV", waits=[("sS", s0v)])
    p.op("vector", lambda v: v.drain(), "sV")
    w = nqt * 128

    def lnred(v, nqt=nqt, w=w):
        v.tensor_reduce(LCOL[:, 8:8 + nqt],
                        bass.AP(TMPA, 0, [[TMPA.ap().ap[0][0], 128], [128, nqt], [1, 128]]),
                        AX.X, OP.add)
        v.drain()
        v.tensor_mul(TMPA[:, 512:512 + w], TMPA[:, 0:w], TMPA[:, 0:w])
        v.drain()
        v.tensor_reduce(LCOL[:, 12:12 + nqt],
                        bass.AP(TMPA, 512, [[TMPA.ap().ap[0][0], 128], [128, nqt], [1, 128]]),
                        AX.X, OP.add)
        v.drain()
        v.tensor_scalar(LCOL[:, 8:8 + nqt], LCOL[:, 8:8 + nqt], 1.0 / 128, None, OP.mult)
        v.tensor_scalar(LCOL[:, 12:12 + nqt], LCOL[:, 12:12 + nqt], 1.0 / 128, None, OP.mult)
        v.drain()
        v.tensor_mul(LCOL[:, 0:nqt], LCOL[:, 8:8 + nqt], LCOL[:, 8:8 + nqt])
        v.drain()
        v.tensor_sub(LCOL[:, 12:12 + nqt], LCOL[:, 12:12 + nqt], LCOL[:, 0:nqt])
        v.drain()
    def addeps(v, nqt=nqt):
        v.tensor_scalar(LCOL[:, 12:12 + nqt], LCOL[:, 12:12 + nqt], LN_EPS, None,
                        OP.add)
        v.drain()
    p.op("vector", lnred, "sV")
    p.op("vector", addeps, "sV")
    p.op("scalar", lambda s, nqt=nqt: s.activation(
        LCOL[:, 12:12 + nqt], LCOL[:, 12:12 + nqt], AF.Sqrt),
        "sS", waits=[("sV", p.val("sV"))])

    def fin(v, nqt=nqt):
        v.reciprocal(LCOL[:, 12:12 + nqt], LCOL[:, 12:12 + nqt])
        v.drain()
        v.tensor_mul(LCOL[:, 8:8 + nqt], LCOL[:, 8:8 + nqt], LCOL[:, 12:12 + nqt])
        v.drain()
        v.tensor_scalar(LCOL[:, 8:8 + nqt], LCOL[:, 8:8 + nqt], -1.0, None, OP.mult)
        v.drain()
    p.op("vector", fin, "sV", waits=[("sS", p.val("sS"))])
    lnv = p.val("sV")

    for qt in range(nqt):
        def app(v, qt=qt):
            sl = TMPA[:, qt * 128:(qt + 1) * 128]
            v.tensor_scalar(sl, sl, LCOL[:, 12 + qt:13 + qt], LCOL[:, 8 + qt:9 + qt],
                            OP.mult, OP.add)
            v.drain()
            v.tensor_mul(sl, sl, g["GLNG"][:])
            v.drain()
            v.tensor_add(TMPB[:, qt * 128:(qt + 1) * 128], sl, g["GLNB"][:])
            v.drain()
        p.op("vector", app, "sV", waits=[("sV", lnv)])
    av = p.val("sV")

    def tr(te, nqt=nqt, av=max(av, g["ptv"]["v"])):
        te.wait_ge(p.sems["sV"], av)
        for qt in range(nqt):
            te.transpose(PT[:, qt * 128:(qt + 1) * 128],
                         TMPB[:, qt * 128:(qt + 1) * 128], CID[:])
    p.op("tensor", tr, "sT")
    tv = p.val("sT")

    def ev(v, c=c, q0=q0, nqt=nqt):
        for qt in range(nqt):
            v.tensor_copy(OPRIME[:, c * NTOK + q0 + qt * 128:
                                 c * NTOK + q0 + (qt + 1) * 128],
                          PT[:, qt * 128:(qt + 1) * 128])
        v.drain()
    p.op("vector", ev, "sV", waits=[("sT", tv)])
    g["ptv"]["v"] = p.val("sV")


def build_head_bound(p, g, QC, KC, PPA, PPB, qk_done):
    PSS, CHALF, SR2 = g["PSS"], g["CHALF"], g["SR2"]
    for (i, SRC) in ((0, QC), (1, KC)):
        def sq(v, SRC=SRC):
            v.tensor_mul(PPA[:, 0:1024], SRC[:, 0:1024], SRC[:, 0:1024])
            v.tensor_mul(PPB[:, 0:1024], SRC[:, 1024:2048], SRC[:, 1024:2048])
            v.drain()
        p.op("vector", sq, "sV", waits=[("sS", qk_done), ("sT", p.val("sT"))])

        for tc in range(4):
            sv = p.val("sV")

            def mm(te, sv=sv, tc=tc):
                te.wait_ge(p.sems["sV"], sv)
                srcpp = PPA if tc < 2 else PPB
                te.matmul(PSS[0:2, 0:512],
                          CHALF[:], srcpp[:, (tc % 2) * 512:(tc % 2 + 1) * 512],
                          start=True, stop=True)
            p.op("tensor", mm, "sT")
            tv = p.val("sT")

            def red(v, i=i, tc=tc):
                v.tensor_reduce(SR2[0:2, 8 + tc:9 + tc], PSS[0:2, 0:512], AX.X, OP.max)
                v.drain()
            p.op("vector", red, "sV", waits=[("sT", tv)])

        def red2(v, i=i):
            v.tensor_reduce(SR2[0:2, i:i + 1], SR2[0:2, 8:12], AX.X, OP.max)
            v.drain()
        p.op("vector", red2, "sV")

    def mul(v):
        v.tensor_mul(SR2[0:2, 2:3], SR2[0:2, 0:1], SR2[0:2, 1:2])
        v.drain()
    p.op("vector", mul, "sV")
    p.op("scalar", lambda s: s.activation(SR2[0:2, 3:4], SR2[0:2, 2:3], AF.Sqrt),
         "sS", waits=[("sV", p.val("sV"))])

    def neg(v):
        v.tensor_scalar(SR2[0:2, 4:5], SR2[0:2, 3:4], -1.0, None, OP.mult)
        v.drain()
        v.stream_shuffle(SR2[0:32, 5:6], SR2[0:32, 4:5], [1] + list(range(1, 32)))
        v.drain()
    p.op("vector", neg, "sV", waits=[("sS", p.val("sS"))])
    vv = p.val("sV")
    bcast_row(p, g, g["LCOL"][:, 16:17], SR2[0:1, 4:5], 1, vv)
    bcast_row(p, g, g["LCOL"][:, 17:18], SR2[0:1, 5:6], 1, vv)


def build_b_smalls(p, g):
    LQK, SR2, LROW = g["LQK"], g["SR2"], g["LROW"]

    def lam(v):
        v.tensor_mul(SR2[0:1, 0:64], LQK[0:1, 0:64], LQK[0:1, 64:128])
        v.tensor_mul(SR2[0:1, 64:128], LQK[0:1, 128:192], LQK[0:1, 192:256])
        v.drain()
        v.tensor_reduce(SR2[0:1, 128:129], SR2[0:1, 0:64], AX.X, OP.add)
        v.tensor_reduce(SR2[0:1, 129:130], SR2[0:1, 64:128], AX.X, OP.add)
        v.drain()
    p.op("vector", lam, "sV")
    p.op("scalar", lambda s: s.activation(SR2[0:1, 130:132], SR2[0:1, 128:130], AF.Exp),
         "sS", waits=[("sV", p.val("sV"))])

    def lam2(v):
        v.tensor_sub(SR2[0:1, 132:133], SR2[0:1, 130:131], SR2[0:1, 131:132])
        v.drain()
        v.tensor_scalar(SR2[0:1, 133:134], SR2[0:1, 132:133], LAM_INIT, -1.0,
                        OP.add, OP.mult)
        v.drain()
    p.op("vector", lam2, "sV", waits=[("sS", p.val("sS"))])
    vv = p.val("sV")
    bcast_row(p, g, g["LAMN"][:], SR2[0:1, 133:134], 1, vv)
    bcast_row(p, g, g["GLNG"][:], LROW[0:1, 0:128], 128, vv)
    bcast_row(p, g, g["GLNB"][:], LROW[0:1, 128:256], 128, vv)


def build_ffn(p, g, first):
    nc, dr = g["nc"], g["dr"]
    PS, PT, PSS, CID = g["PS"], g["PT"], g["PSS"], g["CID"]

    p.barrier()

    with ExitStack() as stack:
        E = stack.enter_context
        sfx = "1" if first else "2"
        FQ = E(nc.sbuf_tensor("FQ" + sfx, [128, 2 * 2048], F32))
        FQI = E(nc.sbuf_tensor("FQI" + sfx, [128, 2048], I32))

        if first:
            with ExitStack() as wstack:
                build_wprep(p, g, wstack, FQ)

        XQT = E(nc.sbuf_tensor("XQT" + sfx, [128, DC * (NTOK + 2)], BF16))
        HS = E(nc.sbuf_tensor("HS" + sfx, [128, HC * TCH], F32))
        HQ = E(nc.sbuf_tensor("HQ" + sfx, [128, HC * TCH], BF16))
        W1TS = E(nc.sbuf_tensor("W1TS" + sfx, [128, 2 * DC * 128], BF16))
        W2TS = E(nc.sbuf_tensor("W2TS" + sfx, [128, 2 * HC * 128], BF16))
        SB = E(nc.sbuf_tensor("SB" + sfx, [128, 256], F32))

        statv = build_ln_stats(p, g)
        pad_ap = bass.AP(XQT, 0, [[XQT.ap().ap[0][0], 128], [NTOK + 2, DC],
                                  [NTOK + 1, 2], [1, 1]])
        p.op("vector", lambda v: v.memset(pad_ap, 0.0), "sV")

        for j in range(TT):
            def xln(v, j=j):
                v.tensor_scalar(g["TMPA"][:], g["X"][:, j * D:(j + 1) * D],
                                g["SSTAT"][:, 32 + j:33 + j],
                                g["SSTAT"][:, 48 + j:49 + j], OP.mult, OP.add)
                v.drain()
            p.op("vector", xln, "sV", waits=[("sV", statv)])

            def qs(v, j=j):
                F = g["FSC"]
                v.tensor_reduce(F[:, j:j + 1], g["TMPA"][:], AX.X, OP.max,
                                apply_absolute_value=True)
                v.drain()
                v.tensor_scalar(F[:, j:j + 1], F[:, j:j + 1], 1e-5, None, OP.max)
                v.drain()
                v.tensor_scalar(F[:, 16 + j:17 + j], F[:, j:j + 1], 1.0 / 127, None,
                                OP.mult)
                v.reciprocal(F[:, 32 + j:33 + j], F[:, j:j + 1])
                v.drain()
                v.tensor_scalar(F[:, 32 + j:33 + j], F[:, 32 + j:33 + j], 127.0,
                                None, OP.mult)
                v.drain()
            p.op("vector", qs, "sV")

            def qq(v, j=j):
                F = g["FSC"]
                v.tensor_scalar(FQ[:, 0:1024], g["TMPA"][:], F[:, 32 + j:33 + j],
                                None, OP.mult)
                v.drain()
                v.tensor_copy(FQI[:, 0:1024], FQ[:, 0:1024])
                v.drain()
                v.tensor_copy(FQ[:, 1024:2048], FQI[:, 0:1024])
                v.drain()
                v.tensor_sub(FQ[:, 2048:3072], FQ[:, 0:1024], FQ[:, 1024:2048])
                v.drain()
                v.scalar_tensor_tensor(FQ[:, 0:1024], FQ[:, 2048:3072], 0.5,
                                       FQ[:, 1024:2048], OP.is_gt, OP.add)
                v.tensor_single_scalar(FQ[:, 3072:4096], FQ[:, 2048:3072], -0.5,
                                       OP.is_lt)
                v.drain()
                v.tensor_sub(FQ[:, 1024:2048], FQ[:, 0:1024], FQ[:, 3072:4096])
                v.drain()
                v.tensor_scalar(g["TMPB"][:], FQ[:, 1024:2048], F[:, 16 + j:17 + j],
                                None, OP.mult)
                v.drain()
            p.op("vector", qq, "sV")
            transpose_to(p, g, XQT, j, DC, NTOK + 2, 1, p.val("sV"))
        xqt_done = p.val("sV")

        w1t_t, w2t_t = dr["w1t"].tensor, dr["w2t"].tensor
        w1_mm_done = {0: 0, 1: 0}
        w2_mm_done = {0: 0, 1: 0}
        conv_done = {0: 0, 1: 0}
        y_evac = {0: 0, 1: 0}

        for tc in range(NTC):
            for ch in range(HC):
                buf = ch % 2
                wsem = "ldwa" if buf == 0 else "ldwb"
                p.op("sync", lambda s, ch=ch, buf=buf: s.dma_start(
                    W1TS[:, buf * 1024:(buf + 1) * 1024],
                    bass.AP(w1t_t, ch * 128, [[DH, 128], [128 * DH, DC], [1, 128]])),
                    wsem, 16, waits=[("sT", w1_mm_done[buf])])
                lwv = p.val(wsem)
                bank = PS[ch % 2]

                def mm(te, tc=tc, buf=buf, bank=bank, lwv=lwv, wsem=wsem,
                       cd=conv_done[ch % 2], xq=xqt_done):
                    te.wait_ge(p.sems[wsem], lwv)
                    te.wait_ge(p.sems["sV"], max(cd, xq))
                    for d in range(DC):
                        te.matmul(bank[:, 0:TCH + 2],
                                  W1TS[:, buf * 1024 + d * 128:buf * 1024 + (d + 1) * 128],
                                  XQT[:, d * (NTOK + 2) + tc * TCH:
                                      d * (NTOK + 2) + tc * TCH + TCH + 2],
                                  start=(d == 0), stop=(d == DC - 1))
                p.op("tensor", mm, "sT")
                w1_mm_done[buf] = p.val("sT")
                tv = p.val("sT")

                def conv(v, ch=ch, bank=bank):
                    CW, CB = g["CONVP"], g["CONVB"]
                    v.tensor_scalar(FQ[:, 3584:3584 + TCH], bank[:, 0:TCH],
                                    CW[:, ch * 3:ch * 3 + 1], CB[:, ch:ch + 1],
                                    OP.mult, OP.add)
                    v.drain()
                    v.scalar_tensor_tensor(FQ[:, 3840:3840 + TCH],
                                           bank[:, 1:1 + TCH],
                                           CW[:, ch * 3 + 1:ch * 3 + 2],
                                           FQ[:, 3584:3584 + TCH], OP.mult, OP.add)
                    v.drain()
                    v.scalar_tensor_tensor(FQ[:, 3584:3584 + TCH],
                                           bank[:, 2:2 + TCH],
                                           CW[:, ch * 3 + 2:ch * 3 + 3],
                                           FQ[:, 3840:3840 + TCH],
                                           OP.mult, OP.add)
                    v.drain()
                p.op("vector", conv, "sV", waits=[("sT", tv), ("sS", p.val("sS"))])
                conv_done[ch % 2] = p.val("sV")
                cv = p.val("sV")
                p.op("scalar", lambda s, ch=ch: s.activation(
                    HS[:, ch * TCH:(ch + 1) * TCH], FQ[:, 3584:3584 + TCH], AF.Silu),
                    "sS", waits=[("sV", cv)])
            hs_done = p.val("sS")

            def amax(v):
                v.tensor_reduce(FQ[:, 3584:3584 + TCH],
                                bass.AP(HS, 0, [[HS.ap().ap[0][0], 128], [1, TCH],
                                                [TCH, HC]]),
                                AX.X, OP.max, apply_absolute_value=True)
                v.drain()
                v.tensor_copy(g["TMPB"][:, 0:TCH], FQ[:, 3584:3584 + TCH])
                v.drain()
            p.op("vector", amax, "sV", waits=[("sS", hs_done)])
            vv = p.val("sV")

            def trm(te, vv=max(vv, g["ptv"]["v"])):
                te.wait_ge(p.sems["sV"], vv)
                for i in range(TCH // 128):
                    te.transpose(PT[:, i * 128:(i + 1) * 128],
                                 g["TMPB"][:, i * 128:(i + 1) * 128], CID[:])
            p.op("tensor", trm, "sT")
            tv = p.val("sT")

            def smax(v, tv=tv):
                F = g["FSC"]
                for i in range(TCH // 128):
                    v.tensor_reduce(F[:, 40 + i:41 + i], PT[:, i * 128:(i + 1) * 128],
                                    AX.X, OP.max)
                v.drain()
                v.tensor_scalar(F[:, 40:42], F[:, 40:42], 1e-5, None, OP.max)
                v.drain()
                v.tensor_scalar(F[:, 44:46], F[:, 40:42], 1.0 / 127, None, OP.mult)
                v.drain()
                v.tensor_scalar(F[:, 44:46], F[:, 44:46], g["PERS"][:, 1:2], None,
                                OP.mult)
                v.drain()
                v.reciprocal(F[:, 40:42], F[:, 40:42])
                v.drain()
                v.tensor_scalar(F[:, 40:42], F[:, 40:42], 127.0, None, OP.mult)
                v.drain()
                v.tensor_copy(g["TMPB"][:, 0:2], F[:, 40:42])
                v.drain()
            p.op("vector", smax, "sV", waits=[("sT", tv)])
            g["ptv"]["v"] = p.val("sV")
            vv2 = p.val("sV")

            def trs(te, vv2=max(vv2, g["ptv"]["v"])):
                te.wait_ge(p.sems["sV"], vv2)
                te.transpose(PT[:, 0:128], g["TMPB"][:, 0:128], CID[:])
            p.op("tensor", trs, "sT")
            tv2 = p.val("sT")

            def srow(v, tv2=tv2):
                v.tensor_copy(g["SR2"][0:2, 128:256], PT[0:2, 0:128])
                v.drain()
                v.stream_shuffle(g["SR2"][0:32, 0:128], g["SR2"][0:32, 128:256],
                                 [1] + list(range(1, 32)))
                v.drain()
            p.op("vector", srow, "sV", waits=[("sT", tv2)])
            g["ptv"]["v"] = p.val("sV")
            vv3 = p.val("sV")

            def bmm(te, vv3=max(vv3, g["pssv"]["v"])):
                te.wait_ge(p.sems["sV"], vv3)
                te.matmul(PSS[:, 0:128], g["CONESF"][0:1, :], g["SR2"][0:1, 128:256],
                          start=True, stop=True)
                te.matmul(PSS[:, 128:256], g["CONESF"][0:1, :], g["SR2"][0:1, 0:128],
                          start=True, stop=True)
            p.op("tensor", bmm, "sT")
            tv3 = p.val("sT")

            def sbfill(v, tv3=tv3):
                v.tensor_copy(SB[:, 0:TCH], PSS[:, 0:TCH])
                v.drain()
            p.op("vector", sbfill, "sV", waits=[("sT", tv3)])
            g["pssv"]["v"] = p.val("sV")

            for gq in range(HC * TCH // 2048):
                o = gq * 2048

                def qz(v, o=o):
                    A, B = FQ[:, 0:2048], FQ[:, 2048:4096]
                    Ho = HQ[:, o:o + 2048]
                    sbb = bass.AP(SB, 0, [[SB.ap().ap[0][0], 128],
                                          [0, 2048 // TCH], [1, TCH]])
                    v.tensor_tensor(A, HS[:, o:o + 2048], sbb, OP.mult)
                    v.drain()
                    v.tensor_copy(FQI[:, 0:2048], A)
                    v.drain()
                    v.tensor_copy(B, FQI[:, 0:2048])
                    v.drain()
                    v.tensor_sub(A, A, B)
                    v.drain()
                    v.scalar_tensor_tensor(B, A, 0.5, B, OP.is_gt, OP.add)
                    v.drain()
                    v.tensor_single_scalar(Ho, A, -0.5, OP.is_lt)
                    v.drain()
                    v.tensor_sub(Ho, B, Ho)
                    v.drain()
                p.op("vector", qz, "sV")
            hq_done = p.val("sV")

            for od in range(DC):
                buf = od % 2
                wsem = "ldwc" if buf == 0 else "ldwd"
                p.op("sync", lambda s, od=od, buf=buf: s.dma_start(
                    W2TS[:, buf * HC * 128:(buf + 1) * HC * 128],
                    bass.AP(w2t_t, od * 128, [[D, 128], [128 * D, HC], [1, 128]])),
                    wsem, 16, waits=[("sT", w2_mm_done[buf])])
                lwv = p.val(wsem)
                bank = PS[2 + od % 2]

                def ymm(te, buf=buf, bank=bank, lwv=lwv, wsem=wsem, hq=hq_done,
                        ye=y_evac[od % 2]):
                    te.wait_ge(p.sems[wsem], lwv)
                    te.wait_ge(p.sems["sV"], hq)
                    te.wait_ge(p.sems["sS"], ye)
                    for ch in range(HC):
                        te.matmul(bank[:, 0:TCH],
                                  W2TS[:, buf * HC * 128 + ch * 128:
                                       buf * HC * 128 + (ch + 1) * 128],
                                  HQ[:, ch * TCH:(ch + 1) * TCH],
                                  start=(ch == 0), stop=(ch == HC - 1))
                p.op("tensor", ymm, "sT")
                w2_mm_done[buf] = p.val("sT")
                tv = p.val("sT")
                p.op("scalar", lambda s, bank=bank: s.activation(
                    g["TMPB"][:, 0:TCH], bank[:, 0:TCH], AF.Copy), "sS",
                    waits=[("sT", tv)])
                y_evac[od % 2] = p.val("sS")
                sv = p.val("sS")

                def tr2(te, sv=sv, pw=g["ptv"]["v"]):
                    te.wait_ge(p.sems["sS"], sv)
                    te.wait_ge(p.sems["sV"], pw)
                    for i in range(TCH // 128):
                        te.transpose(PT[:, 512 + i * 128:512 + (i + 1) * 128],
                                     g["TMPB"][:, i * 128:(i + 1) * 128], CID[:])
                p.op("tensor", tr2, "sT")
                tv2 = p.val("sT")

                def add(v, tc=tc, od=od):
                    for i in range(TCH // 128):
                        j = tc * (TCH // 128) + i
                        sl = g["X"][:, j * D + od * 128:j * D + (od + 1) * 128]
                        v.scalar_tensor_tensor(
                            sl, PT[:, 512 + i * 128:512 + (i + 1) * 128],
                            g["FSC"][:, 44 + i:45 + i], sl, OP.mult, OP.add)
                    v.drain()
                p.op("vector", add, "sV", waits=[("sT", tv2)])
                g["ptv"]["v"] = p.val("sV")


def build_wprep(p, g, stack, FQ):
    nc, dr = g["nc"], g["dr"]
    E = stack.enter_context
    WPF = E(nc.sbuf_tensor("WPF", [128, 1024], F32))
    WPB = E(nc.sbuf_tensor("WPB", [128, 1024], BF16))
    WTS = E(nc.sbuf_tensor("WTS", [128, 1024], BF16))
    ACC = E(nc.sbuf_tensor("ACC", [128, 4], F32))
    PT, PSS, SR2 = g["PT"], g["PSS"], g["SR2"]

    p.op("vector", lambda v: (v.memset(ACC[:, 0:2], 0.0), v.drain())[1], "sV")

    loads = []
    for (wi, nm) in ((0, "w1"), (1, "w2")):
        wt = dr[nm].tensor
        rows, cols = (DH, D) if wi == 0 else (D, DH)
        for r in range(rows // 128):
            for s_ in range(cols // 1024):
                loads.append((wi, wt, r * 128 * cols + s_ * 1024, cols, r, s_))

    for (wi, wt, off, cols, r, s_) in loads:
        lv = p.val("sV")
        p.op("sync", lambda s, wt=wt, off=off, cols=cols: s.dma_start(
            WPF[:], bass.AP(wt, off, [[cols, 128], [1, 1024]])), "ldw", 16,
            waits=[("sV", lv)])

        def acc(v, wi=wi):
            v.tensor_reduce(ACC[:, 2 + wi:3 + wi], WPF[:], AX.X, OP.add,
                            apply_absolute_value=True)
            v.drain()
            v.tensor_add(ACC[:, wi:wi + 1], ACC[:, wi:wi + 1], ACC[:, 2 + wi:3 + wi])
            v.drain()
        p.op("vector", acc, "sV", waits=[("ldw", 16 * p.val("ldw") // 16)])
    accv = p.val("sV")

    def tot(te, accv=max(accv, g["pssv"]["v"])):
        te.wait_ge(p.sems["sV"], accv)
        te.matmul(PSS[0:1, 0:1], g["CONESCOL"][:], ACC[:, 0:1], start=True, stop=True)
        te.matmul(PSS[0:1, 1:2], g["CONESCOL"][:], ACC[:, 1:2], start=True, stop=True)
    p.op("tensor", tot, "sT")
    tv = p.val("sT")

    def scal(v, tv=tv):
        S = SR2
        v.tensor_copy(S[0:1, 0:2], PSS[0:1, 0:2])
        v.drain()
        v.tensor_scalar(S[0:1, 2:4], S[0:1, 0:2], 1.0 / (DH * D), None, OP.mult)
        v.drain()
        v.tensor_scalar(S[0:1, 2:4], S[0:1, 2:4], 1e-5, None, OP.max)
        v.drain()
        v.reciprocal(S[0:1, 4:6], S[0:1, 2:4])
        v.drain()
    p.op("vector", scal, "sV", waits=[("sT", tv)])
    g["pssv"]["v"] = p.val("sV")
    vv = p.val("sV")
    bcast_row(p, g, g["PERS"][:, 0:1], SR2[0:1, 4:5], 1, vv)
    bcast_row(p, g, g["PERS"][:, 1:2], SR2[0:1, 3:4], 1, vv)
    bcast_row(p, g, g["PERS"][:, 2:3], SR2[0:1, 5:6], 1, vv)
    bcast_row(p, g, g["PERS"][:, 3:4], SR2[0:1, 2:3], 1, vv)
    bcv = p.val("sV")

    def cf(v):
        v.tensor_scalar(g["CONVP"][:], g["CONVP"][:], g["PERS"][:, 3:4], None, OP.mult)
        v.drain()
    p.op("vector", cf, "sV", waits=[("sV", bcv)])

    for (wi, wt, off, cols, r, s_) in loads:
        lv = p.val("sV")
        p.op("sync", lambda s, wt=wt, off=off, cols=cols: s.dma_start(
            WPF[:], bass.AP(wt, off, [[cols, 128], [1, 1024]])), "ldw", 16,
            waits=[("sV", lv)])

        def quant(v, wi=wi):
            col = 0 if wi == 0 else 2
            v.tensor_scalar(FQ[:, 0:1024], WPF[:], g["PERS"][:, col:col + 1],
                            None, OP.mult)
            v.drain()
            v.tensor_single_scalar(FQ[:, 1024:2048], FQ[:, 0:1024], 0.5, OP.is_gt)
            v.tensor_single_scalar(FQ[:, 2048:3072], FQ[:, 0:1024], -0.5, OP.is_lt)
            v.drain()
            v.tensor_sub(WPB[:], FQ[:, 1024:2048], FQ[:, 2048:3072])
            v.drain()
        p.op("vector", quant, "sV", waits=[("ldw", 16 * p.val("ldw") // 16),
                                           ("sT", p.val("sT"))])
        qv = p.val("sV")

        def tr(te, qv=max(qv, g["ptv"]["v"])):
            te.wait_ge(p.sems["sV"], qv)
            for dcc in range(8):
                te.transpose(PT[:, dcc * 128:(dcc + 1) * 128],
                             WPB[:, dcc * 128:(dcc + 1) * 128], g["CID"][:])
        p.op("tensor", tr, "sT")
        tvq = p.val("sT")
        p.op("vector", lambda v, tvq=tvq: (v.tensor_copy(WTS[:], PT[:, 0:1024]),
                                           v.drain())[1],
             "sV", waits=[("sT", tvq), ("ldst", p.val("ldst"))])
        g["ptv"]["v"] = p.val("sV")
        ev = p.val("sV")
        if wi == 0:
            dt, dcols = dr["w1t"].tensor, DH
        else:
            dt, dcols = dr["w2t"].tensor, D
        dap = bass.AP(dt, (s_ * 1024) * dcols + r * 128,
                      [[dcols, 128], [128 * dcols, 8], [1, 128]])
        p.op("sync", lambda s, dap=dap: s.dma_start(dap, WTS[:]), "ldst", 16,
             waits=[("sV", ev)])
    p.op("sync", lambda s: None, waits=[("ldst", p.val("ldst"))])
    p.barrier()


_CACHED = {}


def _constants():
    cid = np.eye(128, dtype=ml_dtypes.bfloat16)
    cmask = np.zeros((128, 896), dtype=ml_dtypes.bfloat16)
    for i in range(128):
        cmask[i, 384 + i:896] = 1.0
    chalf = np.zeros((128, 2), dtype=ml_dtypes.bfloat16)
    chalf[0:64, 0] = 1.0
    chalf[64:128, 1] = 1.0
    conesf = np.ones((1, 128), dtype=np.float32)
    conescol = np.ones((128, 1), dtype=np.float32)
    return cid, cmask, chalf, conesf, conescol


def kernel(L0_attn_a_inputs, L0_attn_a_W_qkv, L0_attn_a_W_out,
           L0_attn_b_Wq, L0_attn_b_Wk, L0_attn_b_Wv, L0_attn_b_Wo,
           L0_attn_b_lq1, L0_attn_b_lk1, L0_attn_b_lq2, L0_attn_b_lk2,
           L0_attn_b_ln_g, L0_attn_b_ln_b,
           L0_ffn_w1, L0_ffn_conv_w, L0_ffn_conv_b, L0_ffn_w2):
    key = SUBLAYERS
    if key not in _CACHED:
        _CACHED[key] = build_nc()
    nc = _CACHED[key]

    cid, cmask, chalf, conesf, conescol = _constants()
    f32 = lambda a: np.ascontiguousarray(np.asarray(a, dtype=np.float32))
    convw = f32(L0_ffn_conv_w).reshape(32, 128, 3).transpose(1, 0, 2).reshape(128, 96)
    convb = f32(L0_ffn_conv_b).reshape(32, 128).T.copy()
    lqk = np.concatenate([f32(L0_attn_b_lq1), f32(L0_attn_b_lk1),
                          f32(L0_attn_b_lq2), f32(L0_attn_b_lk2)]).reshape(1, 256)
    shared = {
        "wqkv": f32(L0_attn_a_W_qkv), "wouta": f32(L0_attn_a_W_out),
        "wq": f32(L0_attn_b_Wq), "wk": f32(L0_attn_b_Wk),
        "wv": f32(L0_attn_b_Wv), "wo": f32(L0_attn_b_Wo),
        "lqk": lqk, "lng": f32(L0_attn_b_ln_g).reshape(1, 128),
        "lnb": f32(L0_attn_b_ln_b).reshape(1, 128),
        "w1": f32(L0_ffn_w1), "convw": convw, "convb": convb,
        "w2": f32(L0_ffn_w2),
        "cid": cid, "cmask": cmask, "chalf": chalf, "conesf": conesf,
        "conescol": conescol,
    }
    x = f32(L0_attn_a_inputs)
    in_maps = [dict(shared, x=np.ascontiguousarray(x[i])) for i in range(NCORES)]
    res = run_bass_kernel_spmd(nc, in_maps, core_ids=list(range(NCORES)))
    out = np.stack([res.results[i]["out"] for i in range(NCORES)], axis=0)
    return out.astype(np.float32)
